# version 02
# baseline (speedup 1.0000x reference)
import numpy as np
from contextlib import ExitStack

import concourse.bass as bass
import concourse.mybir as mybir
from concourse.bass_utils import run_bass_kernel_spmd

F32 = mybir.dt.float32
BF16 = mybir.dt.bfloat16
AF = mybir.ActivationFunctionType
OP = mybir.AluOpType
AX = mybir.AxisListType

T = 2048
D = 1024
C = 1032
SD = 8
I = 2816
NTH = 8
CG = 704
NCG = I // CG
KD = D // 128
KI = I // 128
EPS = 1e-5


class Prog:

    def __init__(self):
        self.ops = []
        self.last_writer = {}
        self.readers = {}

    def add(self, stream, fn, reads=(), writes=(), dma=False, tag="",
            big=False):
        idx = len(self.ops)
        deps = set()
        for k in reads:
            j = self.last_writer.get(k)
            if j is not None:
                deps.add(j)
        for k in writes:
            j = self.last_writer.get(k)
            if j is not None:
                deps.add(j)
            deps.update(self.readers.get(k, ()))
        deps.discard(idx)
        self.ops.append(dict(stream=stream, fn=fn,
                             sem=stream + ("q" if dma else ""),
                             step=16 if dma else 1, deps=deps, dma=dma,
                             tag=tag, need=False, rank=None, waits=None,
                             big=big, drain=False))
        for k in reads:
            self.readers.setdefault(k, []).append(idx)
        for k in writes:
            self.last_writer[k] = idx
            self.readers[k] = []
        return idx

    def plan(self, force_need=()):
        ops = self.ops
        for op in ops:
            for j in op["deps"]:
                dj = ops[j]
                if dj["stream"] != op["stream"] or dj["dma"]:
                    dj["need"] = True
                if (not dj["dma"] and dj["stream"] not in ("pe", "sp")
                        and not dj["big"]):
                    dj["drain"] = True
        for i in force_need:
            ops[i]["need"] = True
        counters = {}
        for op in ops:
            if op["need"]:
                counters[op["sem"]] = counters.get(op["sem"], 0) + 1
                op["rank"] = counters[op["sem"]]
        self.sem_totals = counters
        know_stream = {}
        know_op = {}
        for i, op in enumerate(ops):
            s = op["stream"]
            know = know_stream.setdefault(s, {})
            waits = []
            for j in sorted(op["deps"]):
                dj = ops[j]
                if dj["stream"] == s and not dj["dma"]:
                    continue
                assert dj["need"] and dj["rank"] is not None, (i, j, dj["tag"])
                if know.get(dj["sem"], 0) >= dj["rank"]:
                    continue
                waits.append((dj["sem"], dj["rank"] * dj["step"]))
                know[dj["sem"]] = dj["rank"]
                for sem2, r2 in know_op.get(j, {}).items():
                    if know.get(sem2, 0) < r2:
                        know[sem2] = r2
            op["waits"] = waits
            snap = dict(know)
            if op["need"]:
                snap[op["sem"]] = max(snap.get(op["sem"], 0), op["rank"])
            know_op[i] = snap

    def emit_stream(self, eng, stream, sems):
        for op in self.ops:
            if op["stream"] != stream:
                continue
            for sem_name, val in op["waits"]:
                eng.wait_ge(sems[sem_name], val)
            instr = op["fn"](eng)
            if op["drain"]:
                instr = eng.drain()
            if op["need"]:
                instr.then_inc(sems[op["sem"]], op["step"])


def build_nc(dbg=False):
    nc = bass.Bass("TRN2", target_bir_lowering=False)

    x_d = nc.dram_tensor("x", [T, D], F32, kind="ExternalInput").ap()
    w_in_d = nc.dram_tensor("w_in", [D, C], F32, kind="ExternalInput").ap()
    b_in_d = nc.dram_tensor("b_in", [1, C], F32, kind="ExternalInput").ap()
    w_g_d = nc.dram_tensor("w_g", [C, SD], F32, kind="ExternalInput").ap()
    b_g_d = nc.dram_tensor("b_g", [1, SD], F32, kind="ExternalInput").ap()
    w_out_d = nc.dram_tensor("w_out", [C, D], F32, kind="ExternalInput").ap()
    b_out_d = nc.dram_tensor("b_out", [1, D], F32, kind="ExternalInput").ap()
    w_gate_d = nc.dram_tensor("w_gate", [I, D], F32, kind="ExternalInput").ap()
    w_up_d = nc.dram_tensor("w_up", [I, D], F32, kind="ExternalInput").ap()
    w_down_d = nc.dram_tensor("w_down", [D, I], F32, kind="ExternalInput").ap()
    out_d = nc.dram_tensor("out", [T, D], F32, kind="ExternalOutput").ap()
    dbg_d = {}
    if dbg:
        for nm, shp, dt in [("dbg_wq", [128, NTH * I], BF16),
                            ("dbg_s1", [128, KI * D], BF16),
                            ("dbg_uq", [128, D], BF16),
                            ("dbg_r1", [T, D], F32),
                            ("dbg_r2", [T, D], F32),
                            ("dbg_cmb", [128, 9 * 128], F32),
                            ("dbg_st", [SD, 128 * 2 * NTH], F32)]:
            dbg_d[nm] = nc.dram_tensor(nm, shp, dt, kind="ExternalOutput").ap()
    wqT_g_d = nc.dram_tensor("wqT_g", [D, I], BF16, kind="Internal").ap()
    wqT_u_d = nc.dram_tensor("wqT_u", [D, I], BF16, kind="Internal").ap()

    es = ExitStack()
    sb = lambda name, shape, dt: es.enter_context(nc.sbuf_tensor(name, shape, dt))
    ps = lambda name, shape, dt: es.enter_context(nc.psum_tensor(name, shape, dt))

    r_sb = sb("r", [128, NTH, D], F32)
    s1 = sb("s1", [128, KI, D], BF16)
    s2 = sb("s2", [128, NTH * I], BF16)
    b1 = sb("b1", [128, D], F32)
    b2 = sb("b2", [128, D], F32)
    hst = sb("hst", [128, 2, CG], F32)
    sg_b = sb("sg", [128, CG], F32)
    s4 = sb("s4", [128, 2, KD, CG], BF16)
    uqT = sb("uqT", [128, NTH, KD, 128], BF16)
    hqT = sb("hqT", [128, KI, 128], BF16)
    u_b = sb("u", [128, D], F32)
    v_b = sb("v", [128, D], BF16)
    ident = sb("ident", [128, 128], BF16)
    ones_b = sb("ones", [1, 128], BF16)
    bin_b = sb("bin_b", [1, C], BF16)
    bg_b = sb("bg_b", [1, SD], BF16)
    bout_b = sb("bout_b", [1, D], BF16)
    wg_b = sb("wg_b", [128, KD, SD], BF16)
    wg8_b = sb("wg8_b", [SD, SD], BF16)
    w8o_b = sb("w8o_b", [SD, D], BF16)
    sums = sb("sums", [128, NTH], F32)
    sqs = sb("sqs", [128, NTH], F32)
    mcol = sb("mcol", [128, NTH], F32)
    varc = sb("varc", [128, NTH], F32)
    stdc = sb("stdc", [128, NTH], F32)
    rstd = sb("rstd", [128, NTH], F32)
    msq = sb("msq", [128, NTH], F32)
    gmax = sb("gmax", [128, I // 16], F32)
    scal = sb("scal", [128, I // 16], F32)
    s2c = sb("s2c", [128, I // 16], F32)
    rinv = sb("rinv", [128, I // 16], F32)
    g_sb = sb("g_sb", [SD, 128], F32)
    si_sb = sb("si_sb", [SD, 128], F32)
    bscan = sb("bscan", [SD, 128], F32)
    states = sb("states", [SD, 128], F32)
    states_b = sb("states_b", [SD, 128], BF16)
    carry = sb("carry", [SD, 1], F32)
    eps_c = sb("eps_c", [128, 1], F32)
    cmb8_b = sb("cmb8_b", [SD, 128], BF16)

    s4flat = s4[:, :, :, :].rearrange("p a b c -> p (a b c)")
    wstage = s4flat.bitcast(F32)[:, 0:I]
    wrow = s4flat[:, 2 * I:2 * I + I]
    AW_OUT = KD * C
    AW_CMB = AW_OUT + KD * D
    s2f = s2[:, :]
    win_b = s2f[:, 0:KD * C].rearrange("p (k c) -> p k c", c=C)
    wout_b = s2f[:, AW_OUT:AW_OUT + KD * D].rearrange("p (k c) -> p k c", c=D)
    cmbT = s2f[:, AW_CMB:AW_CMB + 8 * 128].rearrange("p (j c) -> p j c", c=128)
    hq_v = s2f.rearrange("p (t i) -> p t i", i=I)
    wq_v = s2f.rearrange("p (r w) -> p r w", w=D)
    wqd_v = s2f.rearrange("p (r w) -> p r w", w=I)

    pA = ps("pA", [128, CG], F32)
    pB = ps("pB", [128, CG], F32)
    pF = ps("pF", [128, D], F32)
    pT = ps("pT", [128, 1024], BF16)
    pT2 = ps("pT2", [128, 1024], BF16)

    P = Prog()
    BIG_TAGS = {"qnorm", "qabs", "qc0", "qstt", "qsgn", "qsm", "qout", "lnsq",
                "lnapply", "silu", "hmul", "resid", "aresid", "utrc", "htrc",
                "vtrc", "wtrc", "wtos1", "awcast", "smcast"}
    dve = lambda fn, reads=(), writes=(), tag="": P.add(
        "dve", fn, reads, writes, tag=tag, big=tag in BIG_TAGS)
    act = lambda fn, reads=(), writes=(), tag="": P.add(
        "act", fn, reads, writes, tag=tag, big=tag in BIG_TAGS)
    pe = lambda fn, reads=(), writes=(), tag="": P.add("pe", fn, reads, writes, tag=tag)
    pool = lambda fn, reads=(), writes=(), tag="": P.add("pool", fn, reads, writes, tag=tag)
    sp_dma = lambda fn, reads=(), writes=(), tag="": P.add("sp", fn, reads, writes, dma=True, tag=tag)
    pool_dma = lambda fn, reads=(), writes=(), tag="": P.add("pool", fn, reads, writes, dma=True, tag=tag)

    out_store_ids = []

    pool(lambda e: e.memset(ident[:, :], 0.0), writes=[("ident",)])
    pool(lambda e: e.affine_select(
        out=ident[:, :], in_=ident[:, :], compare_op=OP.not_equal,
        fill=1.0, base=0, pattern=[[-1, 128]], channel_multiplier=1),
        reads=[("ident",)], writes=[("ident",)])
    pool(lambda e: e.memset(ones_b[:, :], 1.0), writes=[("ones",)])
    pool(lambda e: e.memset(carry[:, :], 0.0), writes=[("carry",)])
    pool(lambda e: e.memset(eps_c[:, :], EPS), writes=[("eps_c",)])

    def load_cast(dst_ap, dst_key, src_dram, rows, cols, stage_rows=1):
        sp_dma(lambda e: e.dma_start(out=u_b[0:stage_rows, 0:cols],
                                     in_=src_dram),
               reads=[("u",)], writes=[("u",)], tag="smload")
        dve(lambda e: e.tensor_copy(dst_ap, u_b[0:stage_rows, 0:cols]),
            reads=[("u",)], writes=[dst_key], tag="smcast")

    sp_dma(lambda e: e.dma_start(out=wstage[0:1, 0:C], in_=b_in_d[:, :]),
           reads=[("s4",)], writes=[("s4",)], tag="binload")
    dve(lambda e: e.tensor_copy(bin_b[:, :], wstage[0:1, 0:C]),
        reads=[("s4",)], writes=[("bin_b",)])
    load_cast(bg_b[:, :], ("bg_b",), b_g_d[:, :], 1, SD)
    load_cast(bout_b[:, :], ("bout_b",), b_out_d[:, :], 1, D)
    sp_dma(lambda e: e.dma_start(
        out=wstage[:, 0:KD * SD].rearrange("p (k s) -> p k s", s=SD),
        in_=w_g_d[0:D, :].rearrange("(k p) s -> p k s", p=128)),
        reads=[("s4",)], writes=[("s4",)], tag="wgload")
    dve(lambda e: e.tensor_copy(
        wg_b[:, :, :], wstage[:, 0:KD * SD].rearrange("p (k s) -> p k s", s=SD)),
        reads=[("s4",)], writes=[("wg_b",)])
    sp_dma(lambda e: e.dma_start(out=wstage[0:SD, 0:SD], in_=w_g_d[D:C, :]),
           reads=[("s4",)], writes=[("s4",)], tag="wg8load")
    dve(lambda e: e.tensor_copy(wg8_b[:, :], wstage[0:SD, 0:SD]),
        reads=[("s4",)], writes=[("wg8_b",)])
    sp_dma(lambda e: e.dma_start(out=wstage[0:SD, 0:D], in_=w_out_d[D:C, :]),
           reads=[("s4",)], writes=[("s4",)], tag="w8oload")
    dve(lambda e: e.tensor_copy(w8o_b[:, :], wstage[0:SD, 0:D]),
        reads=[("s4",)], writes=[("w8o_b",)])

    def quant(src_ap, src_key, width, out_ap, out_key, goff):
        G = width // 16
        gm = gmax[:, goff:goff + G]
        sc = scal[:, goff:goff + G]
        s2s = s2c[:, goff:goff + G]
        ri = rinv[:, goff:goff + G]
        src_g = src_ap.rearrange("p (g k) -> p g k", k=16)
        w1 = b1[:, 0:width]
        w2 = b2[:, 0:width]
        w1g = w1.rearrange("p (g k) -> p g k", k=16)
        w2g = w2.rearrange("p (g k) -> p g k", k=16)
        rib = ri.unsqueeze(2).broadcast_to([128, G, 16])
        s2b = s2s.unsqueeze(2).broadcast_to([128, G, 16])
        dve(lambda e: e.tensor_reduce(gm, src_g, axis=AX.X, op=OP.max,
                                      apply_absolute_value=True),
            reads=[src_key], writes=[("gmax",)], tag="qred")
        dve(lambda e: e.tensor_scalar(sc, gm, 1.0 / 6.0, 1e-12,
                                      op0=OP.mult, op1=OP.max),
            reads=[("gmax",)], writes=[("scal",)])
        dve(lambda e: e.tensor_scalar(s2s, sc, 0.5, 0.0, op0=OP.mult, op1=OP.add),
            reads=[("scal",)], writes=[("s2c",)])
        dve(lambda e: e.reciprocal(ri, sc), reads=[("scal",)], writes=[("rinv",)])
        dve(lambda e: e.tensor_tensor(w1g, src_g, rib, op=OP.mult),
            reads=[src_key, ("rinv",)], writes=[("qb1",)], tag="qnorm")
        act(lambda e: e.activation(out=w2, in_=w1, func=AF.Abs),
            reads=[("qb1",)], writes=[("qb2",)], tag="qabs")
        dve(lambda e: e.tensor_scalar(w1, w2, 5.0, 4.0, op0=OP.is_gt, op1=OP.mult),
            reads=[("qb2",)], writes=[("qb1",)], tag="qc0")
        for m in (2.5, 2.5, 3.5, 3.5, 0.25, 0.75, 1.25, 1.75):
            dve(lambda e, m=m: e.scalar_tensor_tensor(
                w1, w2, m, w1, op0=OP.is_gt, op1=OP.add),
                reads=[("qb2",), ("qb1",)], writes=[("qb1",)], tag="qstt")
        act(lambda e: e.activation(out=w2, in_=src_ap, func=AF.Sign),
            reads=[src_key], writes=[("qb2",)], tag="qsgn")
        dve(lambda e: e.tensor_tensor(w2g, w2g, s2b, op=OP.mult),
            reads=[("qb2",), ("s2c",)], writes=[("qb2",)], tag="qsm")
        dve(lambda e: e.tensor_tensor(out_ap, w1, w2, op=OP.mult),
            reads=[("qb1",), ("qb2",)], writes=[out_key], tag="qout")

    def wq_weight(w_d, nrt, wcols, to_hbm_ap, hbm_key, to_sbuf):
        stg_v = wq_v if wcols == D else wqd_v
        for r in range(nrt):
            sp_dma(lambda e, r=r: e.dma_start(
                out=wstage[:, 0:wcols], in_=w_d[r * 128:(r + 1) * 128, :]),
                reads=[("s4",)], writes=[("s4",)], tag="wload")
            off = 0
            while off < wcols:
                sub_w = min(D, wcols - off)
                quant(wstage[:, off:off + sub_w], ("s4",), sub_w,
                      stg_v[:, r, off:off + sub_w], ("s2",), off // 16)
                off += sub_w
        nkc = wcols // 128
        for k in range(nkc):
            for r in range(nrt):
                grp = (r // 8) % 2
                pt = pT if grp == 0 else pT2
                pk = ("pT",) if grp == 0 else ("pT2",)
                pe(lambda e, k=k, r=r, pt=pt: e.transpose(
                    pt[:, (r % 8) * 128:(r % 8 + 1) * 128],
                    stg_v[:, r, k * 128:(k + 1) * 128], ident[:, :]),
                   reads=[("s2",), ("ident",)], writes=[pk], tag="wtr")
                if r % 8 == 7 or r == nrt - 1:
                    lo = (r // 8) * 8
                    n = r - lo + 1
                    act(lambda e, lo=lo, n=n, pt=pt: e.copy(
                        wrow[:, lo * 128:(lo + n) * 128], pt[:, 0:n * 128]),
                        reads=[pk, ("s4",)], writes=[("s4",)], tag="wtrc")
            if to_hbm_ap is not None:
                sp_dma(lambda e, k=k: e.dma_start(
                    out=to_hbm_ap[k * 128:(k + 1) * 128, :],
                    in_=wrow[:, 0:nrt * 128]),
                    reads=[("s4",)], writes=[hbm_key], tag="wstore")
            else:
                dve(lambda e, k=k: e.tensor_copy(
                    to_sbuf[:, k, :], wrow[:, 0:nrt * 128]),
                    reads=[("s4",)], writes=[("s1",)], tag="wtos1")

    wq_weight(w_gate_d, KI, D, wqT_g_d, ("hbm_wg",), None)
    if dbg:
        out_store_ids.append(pool_dma(
            lambda e: e.dma_start(out=dbg_d["dbg_wq"][:, :], in_=s2f),
            reads=[("s2",)], tag="dbgwq"))
    wq_weight(w_up_d, KI, D, wqT_u_d, ("hbm_wu",), None)
    wq_weight(w_down_d, KD, I, None, None, s1)
    if dbg:
        out_store_ids.append(pool_dma(
            lambda e: e.dma_start(
                out=dbg_d["dbg_s1"][:, :],
                in_=s1[:, :, :].rearrange("p a b -> p (a b)")),
            reads=[("s1",)], tag="dbgs1"))

    def ln_stats():
        for t in range(NTH):
            dve(lambda e, t=t: e.tensor_reduce(
                sums[:, t:t + 1], r_sb[:, t, :], axis=AX.X, op=OP.add),
                reads=[("r", t)], writes=[("sums",)], tag="lnsum")
            act(lambda e, t=t: e.activation(
                out=b1[:, 0:D], in_=r_sb[:, t, :], func=AF.Square),
                reads=[("r", t)], writes=[("qb1",)], tag="lnsq")
            dve(lambda e, t=t: e.tensor_reduce(
                sqs[:, t:t + 1], b1[:, 0:D], axis=AX.X, op=OP.add),
                reads=[("qb1",)], writes=[("sqs",)], tag="lnsq2")
        dve(lambda e: e.tensor_scalar(mcol[:, :], sums[:, :], 1.0 / D, 0.0,
                                      op0=OP.mult, op1=OP.add),
            reads=[("sums",)], writes=[("mcol",)])
        dve(lambda e: e.tensor_scalar(varc[:, :], sqs[:, :], 1.0 / D, 0.0,
                                      op0=OP.mult, op1=OP.add),
            reads=[("sqs",)], writes=[("varc",)])
        dve(lambda e: e.tensor_tensor(msq[:, :], mcol[:, :], mcol[:, :],
                                      op=OP.mult),
            reads=[("mcol",)], writes=[("msq",)])
        dve(lambda e: e.tensor_tensor(varc[:, :], varc[:, :], msq[:, :],
                                      op=OP.subtract),
            reads=[("varc",), ("msq",)], writes=[("varc",)])
        act(lambda e: e.activation(out=stdc[:, :], in_=varc[:, :],
                                   func=AF.Sqrt, bias=eps_c[:, :], scale=1.0),
            reads=[("varc",), ("eps_c",)], writes=[("stdc",)], tag="lnsqrt")
        dve(lambda e: e.reciprocal(rstd[:, :], stdc[:, :]),
            reads=[("stdc",)], writes=[("rstd",)])

    def ffn_call(half, is_last):
        ln_stats()
        for t in range(NTH):
            dve(lambda e, t=t: e.tensor_scalar(
                u_b[:, :], r_sb[:, t, :], mcol[:, t:t + 1], rstd[:, t:t + 1],
                op0=OP.subtract, op1=OP.mult),
                reads=[("r", t), ("mcol",), ("rstd",)], writes=[("u",)],
                tag="lnapply")
            quant(u_b[:, :], ("u",), D, v_b[:, :], ("v",), 0)
            if dbg and half == 0 and t == 0 and not is_last and dbg_d.get("dbg_uq") is not None:
                _uq_ap = dbg_d["dbg_uq"]
                out_store_ids.append(pool_dma(
                    lambda e, _a=_uq_ap: e.dma_start(out=_a[:, :], in_=v_b[:, :]),
                    reads=[("v",)], tag="dbguq"))
                dbg_d["dbg_uq"] = None
            for k in range(KD):
                pe(lambda e, k=k: e.transpose(
                    pT[:, k * 128:(k + 1) * 128],
                    v_b[:, k * 128:(k + 1) * 128], ident[:, :]),
                   reads=[("v",), ("ident",)], writes=[("pT",)], tag="utr")
            act(lambda e, t=t: e.copy(
                uqT[:, t, :, :].rearrange("p a b -> p (a b)"), pT[:, :]),
                reads=[("pT",)], writes=[("uqT", t)], tag="utrc")
        for cg in range(NCG):
            for wi, wsrc, hk in ((0, wqT_g_d, ("hbm_wg",)),
                                 (1, wqT_u_d, ("hbm_wu",))):
                sp_dma(lambda e, wi=wi, wsrc=wsrc, cg=cg: e.dma_start(
                    out=s4[:, wi, :, :],
                    in_=wsrc[:, cg * CG:(cg + 1) * CG].rearrange(
                        "(k p) n -> p k n", p=128)),
                    reads=[("s4",), hk], writes=[("s4",)], tag="wchunk")
            for t in range(NTH):
                for wi, pd, pk in ((0, pA, ("pA",)), (1, pB, ("pB",))):
                    for k in range(KD):
                        for a, bnd in ((0, 512), (512, CG)):
                            pe(lambda e, wi=wi, pd=pd, t=t, k=k, a=a, bnd=bnd:
                               e.matmul(pd[:, a:bnd], uqT[:, t, k, :],
                                        s4[:, wi, k, a:bnd],
                                        start=(k == 0), stop=(k == KD - 1)),
                               reads=[("uqT", t), ("s4",)], writes=[pk],
                               tag="gumm")
                act(lambda e: e.activation(out=sg_b[:, :], in_=pA[:, :],
                                           func=AF.Silu),
                    reads=[("pA",)], writes=[("sg",)], tag="silu")
                dve(lambda e, t=t: e.tensor_tensor(
                    hst[:, t % 2, :], sg_b[:, :], pB[:, :], op=OP.mult),
                    reads=[("sg",), ("pB",)], writes=[("hst",)], tag="hmul")
                quant(hst[:, t % 2, :], ("hst",), CG,
                      hq_v[:, t, cg * CG:(cg + 1) * CG], ("s2",),
                      cg * CG // 16)
        for t in range(NTH):
            for k in range(KI):
                grp = (k // 8) % 2
                pt = pT if grp == 0 else pT2
                pk = ("pT",) if grp == 0 else ("pT2",)
                pe(lambda e, k=k, t=t, pt=pt: e.transpose(
                    pt[:, (k % 8) * 128:(k % 8 + 1) * 128],
                    hq_v[:, t, k * 128:(k + 1) * 128], ident[:, :]),
                   reads=[("s2",), ("ident",)], writes=[pk], tag="htr")
                if k % 8 == 7 or k == KI - 1:
                    lo = (k // 8) * 8
                    n = k - lo + 1
                    act(lambda e, lo=lo, n=n, pt=pt: e.copy(
                        hqT[:, lo:lo + n, :].rearrange("p a b -> p (a b)"),
                        pt[:, 0:n * 128]),
                        reads=[pk], writes=[("hqT",)], tag="htrc")
            for k in range(KI):
                for a in (0, 512):
                    pe(lambda e, k=k, a=a: e.matmul(
                        pF[:, a:a + 512], hqT[:, k, :], s1[:, k, a:a + 512],
                        start=(k == 0), stop=(k == KI - 1)),
                       reads=[("hqT",), ("s1",)], writes=[("pF",)], tag="dnmm")
            dve(lambda e, t=t: e.scalar_tensor_tensor(
                r_sb[:, t, :], pF[:, :], 0.5, r_sb[:, t, :],
                op0=OP.mult, op1=OP.add),
                reads=[("pF",), ("r", t)], writes=[("r", t)], tag="resid")
            if is_last:
                off = half * NTH * 128
                oid = pool_dma(lambda e, t=t, off=off: e.dma_start(
                    out=out_d[off + t * 128: off + (t + 1) * 128, :],
                    in_=r_sb[:, t, :]),
                    reads=[("r", t)], writes=[("hbm_out", half, t)],
                    tag="outst")
                out_store_ids.append(oid)

    def attn_call(dbg_half=-1, dbg_dd=None):
        ln_stats()
        for k in range(KD):
            sp_dma(lambda e, k=k: e.dma_start(
                out=wstage[:, 0:C],
                in_=w_in_d[k * 128:(k + 1) * 128, :]),
                reads=[("s4",)], writes=[("s4",)], tag="awload")
            dve(lambda e, k=k: e.tensor_copy(win_b[:, k, :], wstage[:, 0:C]),
                reads=[("s4",)], writes=[("s2",)], tag="awcast")
            sp_dma(lambda e, k=k: e.dma_start(
                out=wstage[:, 0:D], in_=w_out_d[k * 128:(k + 1) * 128, :]),
                reads=[("s4",)], writes=[("s4",)], tag="awload")
            dve(lambda e, k=k: e.tensor_copy(wout_b[:, k, :], wstage[:, 0:D]),
                reads=[("s4",)], writes=[("s2",)], tag="awcast")
        for t in range(NTH):
            dve(lambda e, t=t: e.tensor_scalar(
                v_b[:, :], r_sb[:, t, :], mcol[:, t:t + 1], rstd[:, t:t + 1],
                op0=OP.subtract, op1=OP.mult),
                reads=[("r", t), ("mcol",), ("rstd",)], writes=[("v",)],
                tag="lnapply")
            for k in range(KD):
                pe(lambda e, k=k: e.transpose(
                    pT[:, k * 128:(k + 1) * 128],
                    v_b[:, k * 128:(k + 1) * 128], ident[:, :]),
                   reads=[("v",), ("ident",)], writes=[("pT",)], tag="vtr")
            act(lambda e: e.copy(
                hqT[:, 0:8, :].rearrange("p a b -> p (a b)"), pT[:, :]),
                reads=[("pT",)], writes=[("hqT",)], tag="vtrc")
            for j in range(9):
                if j < 4:
                    reg, pk = pA[:, j * 128:(j + 1) * 128], ("pA",)
                elif j < 8:
                    reg, pk = pB[:, (j - 4) * 128:(j - 3) * 128], ("pB",)
                else:
                    reg, pk = pF[0:SD, 0:128], ("pF",)
                for k in range(KD):
                    lh_ap = (win_b[:, k, j * 128:(j + 1) * 128] if j < 8
                             else win_b[:, k, D:C])
                    pe(lambda e, reg=reg, lh_ap=lh_ap, k=k: e.matmul(
                        reg, lh_ap, hqT[:, k, :], start=(k == 0), stop=False),
                       reads=[("s2",), ("hqT",)], writes=[pk], tag="cmbmm")
                b_ap = bin_b[:, j * 128:(j + 1) * 128] if j < 8 else bin_b[:, D:C]
                pe(lambda e, reg=reg, b_ap=b_ap: e.matmul(
                    reg, b_ap, ones_b[:, :], start=False, stop=True),
                   reads=[("bin_b",), ("ones",)], writes=[pk], tag="cmbbias")
            for j in range(8):
                src = (pA[:, j * 128:(j + 1) * 128] if j < 4
                       else pB[:, (j - 4) * 128:(j - 3) * 128])
                pk = ("pA",) if j < 4 else ("pB",)
                act(lambda e, j=j, src=src: e.copy(cmbT[:, j, :], src),
                    reads=[pk], writes=[("s2",)], tag="cmbev")
            act(lambda e: e.copy(cmb8_b[:, :], pF[0:SD, 0:128]),
                reads=[("pF",)], writes=[("cmb8_b",)], tag="cmb8ev")
            if dbg_half == 0 and t == 0:
                for j in range(8):
                    out_store_ids.append(pool_dma(
                        lambda e, j=j: e.dma_start(
                            out=dbg_dd["dbg_cmb"][:, j * 128:(j + 1) * 128],
                            in_=cmbT[:, j, :]),
                        reads=[("s2",)], tag="dbgcmb"))
                out_store_ids.append(pool_dma(
                    lambda e: e.dma_start(
                        out=dbg_dd["dbg_cmb"][0:SD, 8 * 128:9 * 128],
                        in_=cmb8_b[:, :]),
                    reads=[("cmb8_b",)], tag="dbgcmb8"))
            dve(lambda e: e.tensor_copy(si_sb[:, :], pF[0:SD, 0:128]),
                reads=[("pF",)], writes=[("si_sb",)], tag="siev")
            greg = pF[0:SD, 128:256]
            for j in range(8):
                pe(lambda e, j=j, greg=greg: e.matmul(
                    greg, wg_b[:, j, :], cmbT[:, j, :],
                    start=(j == 0), stop=False),
                   reads=[("wg_b",), ("s2",)], writes=[("pF",)], tag="gmm")
            pe(lambda e, greg=greg: e.matmul(
                greg, wg8_b[:, :], cmb8_b[:, :], start=False, stop=False),
               reads=[("wg8_b",), ("cmb8_b",)], writes=[("pF",)], tag="gmm8")
            pe(lambda e, greg=greg: e.matmul(
                greg, bg_b[:, :], ones_b[:, :], start=False, stop=True),
               reads=[("bg_b",), ("ones",)], writes=[("pF",)], tag="gbias")
            act(lambda e, greg=greg: e.activation(
                out=g_sb[:, :], in_=greg, func=AF.Sigmoid),
                reads=[("pF",)], writes=[("g_sb",)], tag="sigm")
            dve(lambda e: e.tensor_scalar(bscan[:, :], g_sb[:, :], -1.0, 1.0,
                                          op0=OP.mult, op1=OP.add),
                reads=[("g_sb",)], writes=[("bscan",)])
            dve(lambda e: e.tensor_tensor(bscan[:, :], bscan[:, :],
                                          si_sb[:, :], op=OP.mult),
                reads=[("bscan",), ("si_sb",)], writes=[("bscan",)])
            dve(lambda e: e.tensor_tensor_scan(
                states[:, :], g_sb[:, :], bscan[:, :], carry[:, :],
                op0=OP.mult, op1=OP.add),
                reads=[("g_sb",), ("bscan",), ("carry",)],
                writes=[("states",)], tag="scan")
            dve(lambda e: e.tensor_copy(carry[:, :], states[:, 127:128]),
                reads=[("states",)], writes=[("carry",)])
            dve(lambda e: e.tensor_copy(states_b[:, :], states[:, :]),
                reads=[("states",)], writes=[("states_b",)])
            if dbg_half >= 0:
                out_store_ids.append(pool_dma(
                    lambda e, t=t, dh=dbg_half: e.dma_start(
                        out=dbg_dd["dbg_st"][:, (dh * NTH + t) * 128:(dh * NTH + t + 1) * 128],
                        in_=states[:, :]),
                    reads=[("states",)], tag="dbgst"))
            for a in (0, 512):
                for j in range(8):
                    pe(lambda e, j=j, a=a: e.matmul(
                        pF[:, a:a + 512], cmbT[:, j, :],
                        wout_b[:, j, a:a + 512], start=(j == 0), stop=False),
                       reads=[("s2",)], writes=[("pF",)], tag="omm")
                pe(lambda e, a=a: e.matmul(
                    pF[:, a:a + 512], states_b[:, :], w8o_b[:, a:a + 512],
                    start=False, stop=False),
                   reads=[("states_b",), ("w8o_b",)], writes=[("pF",)],
                   tag="omm8")
                pe(lambda e, a=a: e.matmul(
                    pF[:, a:a + 512], ones_b[:, :], bout_b[:, a:a + 512],
                    start=False, stop=True),
                   reads=[("ones",), ("bout_b",)], writes=[("pF",)],
                   tag="obias")
            dve(lambda e, t=t: e.tensor_tensor(
                r_sb[:, t, :], pF[:, :], r_sb[:, t, :], op=OP.add),
                reads=[("pF",), ("r", t)], writes=[("r", t)], tag="aresid")

    for half in range(2):
        for t in range(NTH):
            off = half * NTH * 128
            sp_dma(lambda e, t=t, off=off: e.dma_start(
                out=r_sb[:, t, :],
                in_=x_d[off + t * 128: off + (t + 1) * 128, :]),
                reads=[("r", t)], writes=[("r", t)], tag="xload")
        ffn_call(half, is_last=False)
        if dbg:
            for t in range(NTH):
                off = half * NTH * 128
                out_store_ids.append(pool_dma(
                    lambda e, t=t, off=off: e.dma_start(
                        out=dbg_d["dbg_r1"][off + t * 128: off + (t + 1) * 128, :],
                        in_=r_sb[:, t, :]),
                    reads=[("r", t)], tag="dbgr1"))
        attn_call(half if dbg else -1, dbg_d)
        if dbg:
            for t in range(NTH):
                off = half * NTH * 128
                out_store_ids.append(pool_dma(
                    lambda e, t=t, off=off: e.dma_start(
                        out=dbg_d["dbg_r2"][off + t * 128: off + (t + 1) * 128, :],
                        in_=r_sb[:, t, :]),
                    reads=[("r", t)], tag="dbgr2"))
        ffn_call(half, is_last=True)

    P.plan(force_need=out_store_ids)
    sem_names = sorted({op["sem"] for op in P.ops if op["need"]})
    sems = {nm: es.enter_context(nc.semaphore(nm)) for nm in sem_names}
    with nc.Block() as block:
        @block.sync
        def _(eng):
            P.emit_stream(eng, "sp", sems)

        @block.vector
        def _(eng):
            P.emit_stream(eng, "dve", sems)

        @block.scalar
        def _(eng):
            P.emit_stream(eng, "act", sems)

        @block.tensor
        def _(eng):
            P.emit_stream(eng, "pe", sems)

        @block.gpsimd
        def _(eng):
            P.emit_stream(eng, "pool", sems)
            for qn in ("spq", "poolq"):
                if qn in P.sem_totals:
                    eng.wait_ge(sems[qn], P.sem_totals[qn] * 16)
    es.close()
    return nc


_NC_CACHE = None


def kernel(L0_attn_x, L0_attn_W_in, L0_attn_b_in, L0_attn_W_g, L0_attn_b_g,
           L0_attn_W_out, L0_attn_b_out, L0_ffn_w_gate, L0_ffn_w_up,
           L0_ffn_w_down):
    global _NC_CACHE
    if _NC_CACHE is None:
        _NC_CACHE = build_nc()
    nc = _NC_CACHE
    f32 = lambda a: np.ascontiguousarray(np.asarray(a, dtype=np.float32))
    shared = {
        "w_in": f32(L0_attn_W_in),
        "b_in": f32(L0_attn_b_in).reshape(1, C),
        "w_g": f32(L0_attn_W_g),
        "b_g": f32(L0_attn_b_g).reshape(1, SD),
        "w_out": f32(L0_attn_W_out),
        "b_out": f32(L0_attn_b_out).reshape(1, D),
        "w_gate": f32(L0_ffn_w_gate),
        "w_up": f32(L0_ffn_w_up),
        "w_down": f32(L0_ffn_w_down),
    }
    in_maps = []
    for c in range(8):
        m = dict(shared)
        m["x"] = f32(L0_attn_x[c])
        in_maps.append(m)
    res = run_bass_kernel_spmd(nc, in_maps, core_ids=list(range(8)))
    return np.stack([res.results[c]["out"] for c in range(8)], axis=0)
